# version 08
# baseline (speedup 1.0000x reference)
import numpy as np
import ml_dtypes
import concourse.bass as bass
import concourse.mybir as mybir
from concourse.bass_utils import run_bass_kernel_spmd

F32 = mybir.dt.float32
BF16 = mybir.dt.bfloat16
I32 = mybir.dt.int32
AF = mybir.ActivationFunctionType
ALU = mybir.AluOpType
AX = mybir.AxisListType

T = 4096
C = 1024
H3 = 3072
NT = T // 128
NCH = C // 128
NP = 8
NW = T // 512
NHT = H3 // 128
NGRP = 4
GHT = NHT // NGRP
INV127 = 1.0 / 127.0
INV127SQ = 1.0 / (127.0 * 127.0)

ENGS = ("T", "V", "S", "Y", "G")


class Sem:
    def __init__(self, nc, name):
        self._cm = nc.semaphore(name)
        self.h = self._cm.__enter__()
        self.n = 0


class Builder:

    def __init__(self, nc):
        self.nc = nc
        self.streams = {e: [] for e in ENGS}
        self.known = {e: {} for e in ENGS}
        self.own = {}

    def op(self, e, fn, waits=(), inc=(), dma_inc=()):
        known = self.known[e]
        waits = list(waits)
        own = self.own.get(e)
        if own is not None:
            waits.append((own, own.n))
        wl = []
        for (s, v) in waits:
            if v <= 0 or known.get(s, -1) >= v:
                continue
            wl.append((s.h, v))
            known[s] = max(known.get(s, -1), v)
        post = {}
        pairs = []
        for (s, n) in inc:
            s.n += n
            pairs.append((s.h, n))
            post[s] = s.n
        for (s, n) in dma_inc:
            s.n += n
            pairs.append((s.h, n))
            post[s] = s.n

        def run(eng, wl=wl, pairs=pairs, fn=fn):
            for (h, v) in wl:
                eng.wait_ge(h, v)
            inst = fn(eng)
            for (h, n) in pairs:
                inst = inst.then_inc(h, n)

        self.streams[e].append(run)
        return post


def build_nc(debug=False):
    nc = bass.Bass("TRN2", target_bir_lowering=False,
                   dynamic_dma_scratch_size=24576)
    ikind = "ExternalOutput" if debug else "Internal"

    x_d = nc.dram_tensor("x", [T, C], F32, kind="ExternalInput").ap()
    w_d = nc.dram_tensor("w", [C, H3], F32, kind="ExternalInput").ap()
    bpt_d = nc.dram_tensor("bpt_in", [128, 24], F32, kind="ExternalInput").ap()
    brow_d = nc.dram_tensor("brow_in", [1, 1024], F32, kind="ExternalInput").ap()
    wg_d = nc.dram_tensor("wg", [H3, C], I32, kind="ExternalInput").ap()
    wu_d = nc.dram_tensor("wu", [H3, C], I32, kind="ExternalInput").ap()
    wd_d = nc.dram_tensor("wd", [C, H3], I32, kind="ExternalInput").ap()
    identb_d = nc.dram_tensor("identb_in", [128, 128], BF16, kind="ExternalInput").ap()
    masks_d = nc.dram_tensor("masks", [4, 128, 512], BF16, kind="ExternalInput").ap()
    out_d = nc.dram_tensor("out", [T, C], F32, kind="ExternalOutput").ap()

    qTh = nc.dram_tensor("qTh", [C, T], BF16, kind=ikind).ap()
    kTh = nc.dram_tensor("kTh", [C, T], BF16, kind=ikind).ap()
    vh = nc.dram_tensor("vh", [128, NT, C], BF16, kind=ikind).ap()
    out1h = nc.dram_tensor("out1h", [T, C], F32, kind=ikind).ap()
    if debug:
        xnTd = nc.dram_tensor("xnTd", [NCH, 128, T], BF16, kind="ExternalOutput").ap()
        accd = nc.dram_tensor("accd", [128, NT * C], BF16, kind="ExternalOutput").ap()
        nstd = nc.dram_tensor("nstd", [NP, 1, 32], F32, kind="ExternalOutput").ap()
        negBd = nc.dram_tensor("negBd", [NP, 128, 1], F32, kind="ExternalOutput").ap()
        out1c = nc.dram_tensor("out1c", [T, C], F32, kind="ExternalOutput").ap()
        hTd = nc.dram_tensor("hTd", [NGRP, 128, GHT, T], BF16, kind="ExternalOutput").ap()
        wdtd = nc.dram_tensor("wdtd", [NGRP, 128, GHT, 1024], BF16, kind="ExternalOutput").ap()

    big0 = nc.alloc_sbuf_tensor("big0", [128, NCH, T], BF16).ap()
    big1 = nc.alloc_sbuf_tensor("big1", [128, 32768], BF16).ap()
    big2 = nc.alloc_sbuf_tensor("big2", [128, 24576], BF16).ap()
    xnstage = nc.alloc_sbuf_tensor("xnstage", [128, 1024], BF16).ap()
    wstage = nc.alloc_sbuf_tensor("wstage", [128, 2, 1024], BF16).ap()
    wgt_t = nc.alloc_sbuf_tensor("wgt_t", [128, 1024], BF16).ap()
    wut_t = nc.alloc_sbuf_tensor("wut_t", [128, 1024], BF16).ap()
    gs = nc.alloc_sbuf_tensor("gs", [128, 512], BF16).ap()
    us = nc.alloc_sbuf_tensor("us", [128, 512], BF16).ap()
    masksb = nc.alloc_sbuf_tensor("masksb", [128, 4, 512], BF16).ap()
    identb = nc.alloc_sbuf_tensor("identb", [128, 128], BF16).ap()
    ones128 = nc.alloc_sbuf_tensor("ones128", [128, 1], BF16).ap()
    ones1 = nc.alloc_sbuf_tensor("ones1", [1, 128], BF16).ap()
    bpt = nc.alloc_sbuf_tensor("bpt", [128, 24], F32).ap()
    browb = nc.alloc_sbuf_tensor("browb", [1, 1024], BF16).ap()
    stats = nc.alloc_sbuf_tensor("stats", [128, 16], F32).ap()
    epsb = nc.alloc_sbuf_tensor("epsb", [128, 1], F32).ap()
    nstat = nc.alloc_sbuf_tensor("nstat", [1, 32], F32).ap()
    negb11 = nc.alloc_sbuf_tensor("negb11", [1, 1], BF16).ap()
    negB = nc.alloc_sbuf_tensor("negB", [128, 1], F32).ap()
    rl = nc.alloc_sbuf_tensor("rl", [128, 8], F32).ap()

    acc = big1[:, 0:32768].rearrange("p (t c) -> p t c", t=NT)
    wdt_g = big1[:, 0:6144].rearrange("p (h d) -> p h d", h=GHT)
    Wv = big2[:, 0:12288].rearrange("p (c n) -> p c n", c=NCH)
    qkstage = big2[:, 12288:14336].rearrange("p (r n) -> p r n", r=4)
    kqv = big2[:, 0:16384].rearrange("p (i t) -> p i t", i=4)
    vbuf = big2[:, 16384:20544].rearrange("p (c n) -> p c n", c=NT)
    pTv = big2[:, 20544:22592].rearrange("p (i n) -> p i n", i=2)
    ovst = big2[:, 22592:23616]
    hT = big2[:, 0:24576].rearrange("p (h t) -> p h t", h=GHT)
    smxA = big1[:, 0:4096].bitcast(F32).rearrange("p (i n) -> p i n", i=2)
    smxC = big2[:, 0:4096].bitcast(F32).rearrange("p (i n) -> p i n", i=2)
    outstC = big2[:, 4096:8192].bitcast(F32).rearrange("p (i n) -> p i n", i=2)
    smxD = big1[:, 16384:20480].bitcast(F32).rearrange("p (i n) -> p i n", i=2)
    outstD = big1[:, 20480:24576].bitcast(F32).rearrange("p (i n) -> p i n", i=2)

    psA = nc.alloc_psum_tensor("psA", [128, 8, 512], F32).ap()

    def bank(i):
        return psA[:, i, :]

    def wide(i):
        return psA[:, i:i + 2, :].rearrange("p a b -> p (a b)")

    def bank_bf(i):
        return psA[:, i, :].bitcast(BF16)

    B = Builder(nc)
    qC = Sem(nc, "qC")
    qX = [Sem(nc, "qX0"), Sem(nc, "qX1")]
    qKq = [Sem(nc, "qK0"), Sem(nc, "qK1")]
    qV = Sem(nc, "qV")
    qW = Sem(nc, "qW")
    qWS = [Sem(nc, "qWS0"), Sem(nc, "qWS1")]
    qST = [Sem(nc, "qS0"), Sem(nc, "qS1"), Sem(nc, "qS2"), Sem(nc, "qS3")]
    qOU = [Sem(nc, "qO0"), Sem(nc, "qO1")]
    sT = Sem(nc, "sT")
    sV = Sem(nc, "sV")
    sS = Sem(nc, "sS")
    sG = Sem(nc, "sG")
    B.own = {"V": sV, "S": sS, "G": sG}

    B.op("Y", lambda e: e.dma_start(out=identb[:, :], in_=identb_d[:, :]), dma_inc=[(qC, 16)])
    for j in range(4):
        dst = masksb[:, j, :]
        src = masks_d[j, :, :]
        B.op("Y", lambda e, dst=dst, src=src: e.dma_start(out=dst, in_=src), dma_inc=[(qC, 16)])
    B.op("Y", lambda e: e.dma_start(out=bpt[:, :], in_=bpt_d[:, :]), dma_inc=[(qC, 16)])
    B.op("G", lambda e: e.dma_start(out=browb[:, :], in_=brow_d[:, :]), dma_inc=[(qC, 16)])
    const_ld = qC.n
    B.op("G", lambda e: e.memset(ones128[:, :], 1.0), inc=[(sG, 1)])
    B.op("G", lambda e: e.memset(ones1[:, :], 1.0), inc=[(sG, 1)])
    B.op("G", lambda e: e.memset(epsb[:, :], 1e-5), inc=[(sG, 1)])
    B.op("G", lambda e: e.memset(vbuf[:, :, 64:65], 1.0), inc=[(sG, 1)])
    B.op("G", lambda e: e.memset(vbuf[:, :, 129:130], 1.0), inc=[(sG, 1)])
    const_ms = sG.n


    a_apply_v = [0] * NT
    a_sq_s = [0] * NT
    a_tr_t = [0] * NT
    a_evac_v = [0] * NT
    a_xld = [0] * NT

    for t in range(NT):
        sl = t % 2
        o = (t % 2) * 8
        xs = smxA[:, sl, :]
        xsrc = x_d[t * 128:(t + 1) * 128, :]
        waits = []
        if t >= 2:
            waits = [(sV, a_apply_v[t - 2])]
        B.op("Y", lambda e, xs=xs, xsrc=xsrc: e.dma_start(out=xs, in_=xsrc),
             waits=waits, dma_inc=[(qX[sl], 16)])
        a_xld[t] = qX[sl].n

        B.op("V", lambda e, xs=xs, d=stats[:, o + 0:o + 1]: e.reduce_sum(d, xs, axis=AX.X),
             waits=[(qX[sl], a_xld[t])], inc=[(sV, 1)])
        B.op("V", lambda e, s=stats[:, o + 0:o + 1], d=stats[:, o + 1:o + 2]:
             e.tensor_scalar(d, s, 1.0 / C, None, ALU.mult), inc=[(sV, 1)])
        p = B.op("V", lambda e, xs=xs, d=pTv[:, 0, :], a=stats[:, o + 2:o + 3]:
                 e.scalar_tensor_tensor(d, xs, 1.0, xs, ALU.mult, ALU.mult, accum_out=a),
                 waits=[(qX[sl], a_xld[t])], inc=[(sV, 1)])
        B.op("V", lambda e, m=stats[:, o + 1:o + 2], d=stats[:, o + 3:o + 4]:
             e.tensor_mul(d, m, m), inc=[(sV, 1)])
        B.op("V", lambda e, ssq=stats[:, o + 2:o + 3], msq=stats[:, o + 3:o + 4],
             d=stats[:, o + 4:o + 5]:
             e.tensor_scalar(d, ssq, 1.0 / C, msq, ALU.mult, ALU.subtract),
             inc=[(sV, 1)])
        pv = sV.n
        B.op("S", lambda e, v=stats[:, o + 4:o + 5], d=stats[:, o + 5:o + 6]:
             e.activation(d, v, AF.Sqrt, bias=epsb[:, 0:1]),
             waits=[(sV, pv), (sG, const_ms)], inc=[(sS, 1)])
        ps = sS.n
        B.op("V", lambda e, s=stats[:, o + 5:o + 6], d=stats[:, o + 6:o + 7]:
             e.reciprocal(d, s), waits=[(sS, ps)], inc=[(sV, 1)])
        waits = [(sT, a_tr_t[t - 1])] if t >= 1 else []
        p = B.op("V", lambda e, xs=xs, m=stats[:, o + 1:o + 2], r=stats[:, o + 6:o + 7]:
                 e.tensor_scalar(xnstage[:, :], xs, m, r, ALU.subtract, ALU.mult),
                 waits=waits, inc=[(sV, 1)])
        a_apply_v[t] = p[sV]
        tb = bank_bf(6 + t % 2)
        waits = [(sV, a_apply_v[t]), (qC, const_ld)]
        if t >= 2:
            waits.append((sV, a_evac_v[t - 2]))
        for c in range(NCH):
            dst = tb[:, c * 128:(c + 1) * 128]
            src = xnstage[:, c * 128:(c + 1) * 128]
            p = B.op("T", lambda e, dst=dst, src=src: e.transpose(dst, src, identb[:, :]),
                     waits=waits, inc=[(sT, 1)])
            waits = []
        a_tr_t[t] = p[sT]
        dstx = big0[:, :, t * 128:(t + 1) * 128]
        srcx = tb.rearrange("p (c w) -> p c w", c=NCH)
        p = B.op("V", lambda e, dstx=dstx, srcx=srcx: e.tensor_copy(dstx, srcx),
                 waits=[(sT, a_tr_t[t])], inc=[(sV, 1)])
        a_evac_v[t] = p[sV]
    a_end_v = sV.n
    if debug:
        for c in range(NCH):
            dst = xnTd[c]
            src = big0[:, c, :]
            B.op("S", lambda e, dst=dst, src=src: e.dma_start(out=dst, in_=src),
                 waits=[(sV, a_end_v)], dma_inc=[(qC, 16)])

    b_cast = [0, 0]
    qk_bank_evac = {}
    stage_dma = [0, 0, 0, 0]
    bmm_i = 0

    for half in range(2):
        waits = [] if half == 0 else [(sT, sT.n)]
        for c in range(NCH):
            dst = Wv[:, c, :]
            src = w_d[c * 128:(c + 1) * 128, half * 1536:(half + 1) * 1536]
            B.op("G", lambda e, dst=dst, src=src: e.dma_start(out=dst, in_=src),
                 waits=waits, dma_inc=[(qW, 16)])
            waits = []
        b_cast[half] = qW.n

        qk_tiles = list(range(12)) if half == 0 else list(range(4))
        for fl in qk_tiles:
            ft = half * 12 + fl
            for twp in range(NW // 2):
                bb = (twp % 2) * 2
                for c in range(NCH):
                    lhsT = Wv[:, c, fl * 128:(fl + 1) * 128]
                    for half_w in range(2):
                        tw = 2 * twp + half_w
                        bx = bb + half_w
                        rhs = big0[:, c, tw * 512:(tw + 1) * 512]
                        waits = [(qW, b_cast[half]), (sV, a_end_v)]
                        if c == 0 and bx in qk_bank_evac:
                            waits.append((sV, qk_bank_evac[bx]))
                        B.op("T", lambda e, lhsT=lhsT, rhs=rhs, o=bank(bx), c=c:
                             e.matmul(o, lhsT, rhs, start=(c == 0), stop=(c == NCH - 1)),
                             waits=waits, inc=[(sT, 1)])
                mm_done = sT.n
                for half_w in range(2):
                    tw = 2 * twp + half_w
                    bx = bb + half_w
                    r = (ft * NW + tw) % 4
                    p = B.op("V", lambda e, d=qkstage[:, r, :], s=bank(bx), bc=bpt[:, ft:ft + 1]:
                             e.tensor_scalar(d, s, bc, None, ALU.add),
                             waits=[(sT, mm_done), (qC, const_ld), (qST[r], stage_dma[r])],
                             inc=[(sV, 1)])
                    qk_bank_evac[bx] = p[sV]
                    ev = p[sV]
                    if ft < 8:
                        dst = qTh[ft * 128:(ft + 1) * 128, tw * 512:(tw + 1) * 512]
                    else:
                        dst = kTh[(ft - 8) * 128:(ft - 8 + 1) * 128, tw * 512:(tw + 1) * 512]
                    p = B.op("S", lambda e, dst=dst, s=qkstage[:, r, :]: e.dma_start(out=dst, in_=s),
                             waits=[(sV, ev)], dma_inc=[(qST[r], 16)])
                    stage_dma[r] = p[qST[r]]

    v_bank_evac = {}
    for tt in range(NT):
        bb = (tt % 2) * 2
        for c in range(NCH):
            lhsT = big0[:, c, tt * 128:(tt + 1) * 128]
            for vc in range(2):
                bx = bb + vc
                rhs = Wv[:, c, 512 + vc * 512:512 + (vc + 1) * 512]
                waits = [(qW, b_cast[1])]
                if c == 0 and bx in v_bank_evac:
                    waits.append((sV, v_bank_evac[bx]))
                B.op("T", lambda e, lhsT=lhsT, rhs=rhs, o=bank(bx), c=c:
                     e.matmul(o, lhsT, rhs, start=(c == 0), stop=False),
                     waits=waits, inc=[(sT, 1)])
        for vc in range(2):
            bx = bb + vc
            B.op("T", lambda e, o=bank(bx), r=browb[0:1, vc * 512:(vc + 1) * 512]:
                 e.matmul(o, ones1[:, :], r, start=False, stop=True),
                 waits=[(qC, const_ld), (sG, const_ms)], inc=[(sT, 1)])
            mm_done = sT.n
            r = (tt * 2 + vc) % 4
            p = B.op("V", lambda e, d=qkstage[:, r, :], s=bank(bx):
                     e.tensor_copy(d, s),
                     waits=[(sT, mm_done), (qST[r], stage_dma[r])], inc=[(sV, 1)])
            v_bank_evac[bx] = p[sV]
            ev = p[sV]
            dst = vh[:, tt, vc * 512:(vc + 1) * 512]
            p = B.op("S", lambda e, dst=dst, s=qkstage[:, r, :]: e.dma_start(out=dst, in_=s),
                     waits=[(sV, ev)], dma_inc=[(qST[r], 16)])
            stage_dma[r] = p[qST[r]]
    b_end_t = sT.n
    b_end_v = sV.n
    b_stores = [(qST[r], qST[r].n) for r in range(4)]

    kq_ld = [0, 0]
    kq_last_t = [b_end_t, b_end_t]
    v_ld = 0
    pv_last_t = b_end_t
    exp_done = {}
    pT_rd = {}
    psT_mult_v = {6: b_end_v, 7: b_end_v}
    ov_evac_v = b_end_v
    acc_done_v = b_end_v

    def load_pair(pr):
        nonlocal v_ld
        pb = pr % 2
        waits = list(b_stores) + [(sT, kq_last_t[pb])]
        dst = kqv[:, 2 * pb, :]
        src = kTh[pr * 128:(pr + 1) * 128, :]
        B.op("Y", lambda e, dst=dst, src=src: e.dma_start(out=dst, in_=src),
             waits=waits, dma_inc=[(qKq[pb], 16)])
        dst = kqv[:, 2 * pb + 1, :]
        src = qTh[pr * 128:(pr + 1) * 128, :]
        B.op("Y", lambda e, dst=dst, src=src: e.dma_start(out=dst, in_=src),
             dma_inc=[(qKq[pb], 16)])
        kq_ld[pb] = qKq[pb].n

    def load_v(pr):
        nonlocal v_ld
        waits = list(b_stores) + [(sT, pv_last_t), (sG, const_ms)]
        for hh in range(2):
            h = 2 * pr + hh
            off = 0 if hh == 0 else 65
            for cq in range(4):
                dst = vbuf[:, cq * 8:(cq + 1) * 8, off:off + 64]
                src = vh[:, cq * 8:(cq + 1) * 8, h * 64:(h + 1) * 64]
                B.op("Y", lambda e, dst=dst, src=src: e.dma_start(out=dst, in_=src),
                     waits=waits, dma_inc=[(qV, 16)])
                waits = []
        v_ld = qV.n

    load_pair(0)
    load_v(0)

    for pr in range(NP):
        pb = pr % 2
        kqk = kqv[:, 2 * pb, :]
        kqq = kqv[:, 2 * pb + 1, :]

        sq_mm_t = {}
        red_v = {}
        for it in range(16):
            ch = it % 8
            src = (kqq if it < 8 else kqk)[:, ch * 512:(ch + 1) * 512]
            sqo = pTv[:, it % 2, 0:512]
            waits = [(qKq[pb], kq_ld[pb])]
            if it >= 2:
                waits.append((sT, sq_mm_t[it - 2]))
            else:
                waits.append((sT, pv_last_t))
            B.op("V", lambda e, d=sqo, s=src: e.tensor_mul(d, s, s),
                 waits=waits, inc=[(sV, 1)])
            sq_done = sV.n
            nb = 6 + it % 2
            waits = [(sV, sq_done), (sV, psT_mult_v[nb]), (sG, const_ms)]
            if it >= 2:
                waits.append((sV, red_v[it - 2]))
            p = B.op("T", lambda e, o=bank(nb)[0:1, :], s=sqo:
                     e.matmul(o, ones128[:, :], s, start=True, stop=True),
                     waits=waits, inc=[(sT, 1)])
            sq_mm_t[it] = p[sT]
            dcol = nstat[0:1, it:it + 1]
            p = B.op("V", lambda e, d=dcol, s=bank(nb)[0:1, :]:
                     e.reduce_max(d, s, axis=AX.X),
                     waits=[(sT, sq_mm_t[it])], inc=[(sV, 1)])
            red_v[it] = p[sV]
        psT_mult_v[6] = red_v[14]
        psT_mult_v[7] = red_v[15]
        B.op("V", lambda e: e.reduce_max(nstat[0:1, 16:17], nstat[0:1, 0:8], axis=AX.X),
             inc=[(sV, 1)])
        B.op("V", lambda e: e.reduce_max(nstat[0:1, 17:18], nstat[0:1, 8:16], axis=AX.X),
             inc=[(sV, 1)])
        B.op("V", lambda e: e.tensor_add(nstat[0:1, 18:19], nstat[0:1, 16:17],
                                         nstat[0:1, 17:18]), inc=[(sV, 1)])
        B.op("V", lambda e: e.tensor_scalar(negb11[:, :], nstat[0:1, 18:19], -1.0 / 16.0,
                                            None, ALU.mult), inc=[(sV, 1)])
        nv_ = sV.n
        B.op("T", lambda e, o=psA[:, 6, 0:1]: e.matmul(o, ones1[:, :], negb11[:, :],
                                                       start=True, stop=True),
             waits=[(sV, nv_)], inc=[(sT, 1)])
        nt_ = sT.n
        p = B.op("V", lambda e, s=psA[:, 6, 0:1]: e.tensor_copy(negB[:, :], s),
                 waits=[(sT, nt_)], inc=[(sV, 1)])
        negB_v = p[sV]
        psT_mult_v[6] = p[sV]
        if debug:
            B.op("S", lambda e, dst=nstd[pr], s=nstat[:, :]: e.dma_start(out=dst, in_=s),
                 waits=[(sV, negB_v)], dma_inc=[(qC, 16)])
            B.op("S", lambda e, dst=negBd[pr], s=negB[:, :]: e.dma_start(out=dst, in_=s),
                 waits=[(sV, negB_v)], dma_inc=[(qC, 16)])

        if pr + 1 < NP:
            load_pair(pr + 1)

        for w in range(NW):
            nk = 4 * (w + 1)
            pv_mm_t = {}
            smm_t = {}

            def emit_smm(ki, w=w, nk=nk, kqk=kqk, kqq=kqq, pb=pb):
                par = ki % 2
                waits = [(qKq[pb], kq_ld[pb])]
                if exp_done.get((w, ki - 2)) is not None:
                    waits.append((sS, exp_done[(w, ki - 2)]))
                elif ki < 2 and exp_done.get("last") is not None:
                    waits.append((sS, exp_done["last"]))
                diag = ki >= 4 * w
                for hh, bx in ((0, 2 * par), (1, 2 * par + 1)):
                    r0 = hh * 64
                    lhsT = kqk[r0:r0 + 64, ki * 128:(ki + 1) * 128]
                    rhs = kqq[r0:r0 + 64, w * 512:(w + 1) * 512]
                    B.op("T", lambda e, o=bank(bx), lhsT=lhsT, rhs=rhs, diag=diag:
                         e.matmul(o, lhsT, rhs, start=True, stop=(not diag)),
                         waits=waits, inc=[(sT, 1)])
                    waits = []
                    if diag:
                        j = ki - 4 * w
                        B.op("T", lambda e, o=bank(bx), m=masksb[:, j, :]:
                             e.matmul(o, identb[:, :], m, start=False, stop=True),
                             inc=[(sT, 1)])
                smm_t[ki] = sT.n

            for ki in range(min(2, nk)):
                emit_smm(ki)
            for ki in range(nk):
                par = ki % 2
                waits = [(sT, smm_t[ki]), (sV, negB_v)]
                if pT_rd.get((w, ki - 2)) is not None:
                    waits.append((sT, pT_rd[(w, ki - 2)]))
                elif ki < 2 and pT_rd.get("last") is not None:
                    waits.append((sT, pT_rd["last"]))
                p = B.op("S", lambda e, d=pTv[:, par, :], s=wide(2 * par):
                         e.activation(d, s, AF.Exp, bias=negB[:, 0:1], scale=0.125),
                         waits=waits, inc=[(sS, 1)])
                exp_done[(w, ki)] = p[sS]
                waits = [(sS, exp_done[(w, ki)]), (qV, v_ld)]
                if ki == 0:
                    waits.append((sV, ov_evac_v))
                for hh, ob in ((0, 4), (1, 5)):
                    lhsT = vbuf[:, ki, hh * 65:(hh + 1) * 65]
                    rhs = pTv[:, par, hh * 512:(hh + 1) * 512]
                    B.op("T", lambda e, o=bank(ob)[0:65, :], lhsT=lhsT, rhs=rhs,
                         ki=ki, nk=nk:
                         e.matmul(o, lhsT, rhs, start=(ki == 0), stop=(ki == nk - 1)),
                         waits=waits, inc=[(sT, 1)])
                    waits = []
                pv_mm_t[ki] = sT.n
                pT_rd[(w, ki)] = sT.n
                if ki + 2 < nk:
                    emit_smm(ki + 2)
            pv_last_t = pv_mm_t[nk - 1]
            pT_rd["last"] = pv_last_t
            exp_done["last"] = exp_done[(w, nk - 1)]

            for hh, ob in ((0, 4), (1, 5)):
                dst = ovst[0:65, hh * 512:(hh + 1) * 512]
                B.op("V", lambda e, dst=dst, s=bank(ob)[0:65, :]: e.tensor_copy(dst, s),
                     waits=[(sT, pv_last_t)], inc=[(sV, 1)])
            ov_evac_v = sV.n
            tb = 6 + w % 2
            tbf = bank_bf(tb)
            waits = [(sV, ov_evac_v), (sV, psT_mult_v[tb])]
            for s8 in range(8):
                hh, bcol = s8 // 4, s8 % 4
                src = ovst[0:65, hh * 512 + bcol * 128: hh * 512 + (bcol + 1) * 128]
                dst = tbf[:, s8 * 66:s8 * 66 + 65]
                B.op("T", lambda e, dst=dst, src=src:
                     e.transpose(dst, src, identb[0:65, 0:65]),
                     waits=waits, inc=[(sT, 1)])
                waits = []
            tr_done = sT.n
            B.op("V", lambda e, s=tbf[:, 64:64 + 8 * 66:66]: e.reciprocal(rl[:, 0:8], s),
                 waits=[(sT, tr_done)], inc=[(sV, 1)])
            for s8 in range(8):
                hh, bcol = s8 // 4, s8 % 4
                h = 2 * pr + hh
                tt = 4 * w + bcol
                dst = acc[:, tt, h * 64:(h + 1) * 64]
                src = tbf[:, s8 * 66:s8 * 66 + 64]
                B.op("V", lambda e, dst=dst, src=src, r=rl[:, s8:s8 + 1]:
                     e.tensor_scalar(dst, src, r, None, ALU.mult), inc=[(sV, 1)])
            psT_mult_v[tb] = sV.n
        kq_last_t[pb] = pv_last_t

        if pr + 1 < NP:
            load_v(pr + 1)
    c_end_t = sT.n
    c_end_s = sS.n
    acc_done_v = sV.n
    if debug:
        B.op("S", lambda e: e.dma_start(out=accd[:, :], in_=big1[:, :]),
             waits=[(sV, acc_done_v)], dma_inc=[(qC, 16)])

    add_v = [0] * NT
    outst_dma = [qOU[0].n, qOU[1].n]
    smx_dma = [0, 0]
    for tt in range(NT):
        sl = tt % 2
        waits = [(sV, acc_done_v), (sT, c_end_t), (sS, c_end_s)] if tt < 2 else [(sV, add_v[tt - 2])]
        dst = smxC[:, sl, :]
        src = x_d[tt * 128:(tt + 1) * 128, :]
        B.op("Y", lambda e, dst=dst, src=src: e.dma_start(out=dst, in_=src),
             waits=waits, dma_inc=[(qX[sl], 16)])
        smx_dma[sl] = qX[sl].n
        p = B.op("V", lambda e, d=outstC[:, sl, :], a=smxC[:, sl, :], b=acc[:, tt, :]:
                 e.tensor_tensor(d, a, b, ALU.add),
                 waits=[(qX[sl], smx_dma[sl]), (qOU[sl], outst_dma[sl])], inc=[(sV, 1)])
        add_v[tt] = p[sV]
        dst = out1h[tt * 128:(tt + 1) * 128, :]
        p = B.op("S", lambda e, dst=dst, s=outstC[:, sl, :]: e.dma_start(out=dst, in_=s),
                 waits=[(sV, add_v[tt])], dma_inc=[(qOU[sl], 16)])
        outst_dma[sl] = p[qOU[sl]]
    cinit_last_v = sV.n
    grp_store_done = [(qOU[0], qOU[0].n), (qOU[1], qOU[1].n)]
    if debug:
        B.op("Y", lambda e: e.dma_start(out=out1c[:, :], in_=out1h[:, :]),
             waits=list(grp_store_done), dma_inc=[(qC, 16)])

    wstage_tr = [c_end_t, c_end_t]
    psb_evac_v = {6: cinit_last_v, 7: cinit_last_v}
    hT_rd_t = c_end_t
    gu_bank_silu = {}
    gu_bank_uev = {}
    down_bank_stt = {}
    silu_done = 0
    hmul_done = {}
    wgt_ev_v = 0
    wut_ev_v = 0
    wdt_ev_v = [0] * GHT
    gu_last_t = {}

    for grp in range(NGRP):
        for j in range(GHT):
            ht = grp * GHT + j
            waits = [(sT, wstage_tr[0])]
            dst = wstage[:, 0, :]
            src = wg_d[ht * 128:(ht + 1) * 128, :]
            B.op("G", lambda e, dst=dst, src=src: e.dma_start(out=dst, in_=src),
                 waits=waits, dma_inc=[(qWS[0], 16)])
            wg_cast = qWS[0].n
            dst = wstage[:, 1, :]
            src = wu_d[ht * 128:(ht + 1) * 128, :]
            B.op("G", lambda e, dst=dst, src=src: e.dma_start(out=dst, in_=src),
                 waits=[(sT, wstage_tr[1])], dma_inc=[(qWS[1], 16)])
            wu_cast = qWS[1].n
            for (slot, cast_q, dstt, evv) in ((0, wg_cast, wgt_t, "g"), (1, wu_cast, wut_t, "u")):
                tb = 6 + slot
                tbf = bank_bf(tb)
                waits = [(qWS[slot], cast_q), (sV, psb_evac_v[tb]),
                         (sV, wgt_ev_v if slot == 0 else wut_ev_v)]
                if j > 0 or grp > 0:
                    waits.append((sT, gu_last_t.get(evv, 0)))
                for dc in range(NCH):
                    dstp = tbf[:, dc * 128:(dc + 1) * 128]
                    srcp = wstage[:, slot, dc * 128:(dc + 1) * 128]
                    B.op("T", lambda e, dstp=dstp, srcp=srcp:
                         e.transpose(dstp, srcp, identb[:, :]),
                         waits=waits, inc=[(sT, 1)])
                    waits = []
                wstage_tr[slot] = sT.n
                trd = sT.n
                p = B.op("V", lambda e, d=dstt, s=tbf: e.tensor_copy(d, s),
                         waits=[(sT, trd)], inc=[(sV, 1)])
                psb_evac_v[tb] = p[sV]
                if slot == 0:
                    wgt_ev_v = p[sV]
                else:
                    wut_ev_v = p[sV]
            for twp in range(NW // 2):
                twA, twB = 2 * twp, 2 * twp + 1
                for (bxa, bxb, wt, wev) in ((0, 2, wgt_t, wgt_ev_v), (1, 3, wut_t, wut_ev_v)):
                    waits = [(sV, wev)]
                    for bx in (bxa, bxb):
                        if gu_bank_silu.get(bx) is not None:
                            waits.append((sS, gu_bank_silu[bx]))
                        if gu_bank_uev.get(bx) is not None:
                            waits.append((sV, gu_bank_uev[bx]))
                    for c in range(NCH):
                        lhsT = wt[:, c * 128:(c + 1) * 128]
                        for bx, tw in ((bxa, twA), (bxb, twB)):
                            rhs = big0[:, c, tw * 512:(tw + 1) * 512]
                            B.op("T", lambda e, o=bank(bx), lhsT=lhsT, rhs=rhs, c=c:
                                 e.matmul(o, lhsT, rhs, start=(c == 0), stop=(c == NCH - 1)),
                                 waits=waits, inc=[(sT, 1)])
                            waits = []
                gu_last_t["g"] = sT.n
                gu_last_t["u"] = sT.n
                gu_done = sT.n
                for (tw, gbx, ubx) in ((twA, 0, 1), (twB, 2, 3)):
                    waits = [(sT, gu_done)]
                    if hmul_done.get("v") is not None:
                        waits.append((sV, hmul_done["v"]))
                    p = B.op("S", lambda e, s=bank(gbx): e.activation(gs[:, :], s, AF.Silu,
                                                                     scale=INV127),
                             waits=waits, inc=[(sS, 1)])
                    gu_bank_silu[gbx] = p[sS]
                    silu_done = p[sS]
                    waits = [(sT, gu_done)]
                    if hmul_done.get("v") is not None:
                        waits.append((sV, hmul_done["v"]))
                    if tw == 0 and j == 0:
                        if grp > 0:
                            waits.append((sT, hT_rd_t))
                        waits.append((qOU[0], outst_dma[0]))
                        waits.append((qOU[1], outst_dma[1]))
                    p = B.op("V", lambda e, s=bank(ubx): e.tensor_copy(us[:, :], s),
                             waits=waits, inc=[(sV, 1)])
                    gu_bank_uev[ubx] = p[sV]
                    dsth = hT[:, j, tw * 512:(tw + 1) * 512]
                    p = B.op("V", lambda e, d=dsth: e.tensor_tensor(d, gs[:, :], us[:, :],
                                                                    ALU.mult),
                             waits=[(sS, silu_done)], inc=[(sV, 1)])
                    hmul_done["v"] = p[sV]
        h_ready_v = sV.n
        wd_r = wd_d.rearrange("(dc p) h -> p dc h", p=128)
        for j in range(GHT):
            hc = grp * GHT + j
            slot = j % 2
            waits = [(sT, wstage_tr[slot])]
            for dcc in range(NCH):
                dst = wstage[:, slot, dcc * 128:(dcc + 1) * 128]
                src = wd_r[:, dcc, hc * 128:(hc + 1) * 128]
                B.op("G", lambda e, dst=dst, src=src: e.dma_start(out=dst, in_=src),
                     waits=waits, dma_inc=[(qWS[slot], 16)])
                waits = []
            wd_cast = qWS[slot].n
            tb = 6 + slot
            tbf = bank_bf(tb)
            waits = [(qWS[slot], wd_cast), (sV, psb_evac_v[tb]), (sV, wdt_ev_v[j])]
            if grp > 0:
                waits.append((sT, hT_rd_t))
            for dc in range(NCH):
                dstp = tbf[:, dc * 128:(dc + 1) * 128]
                srcp = wstage[:, slot, dc * 128:(dc + 1) * 128]
                B.op("T", lambda e, dstp=dstp, srcp=srcp:
                     e.transpose(dstp, srcp, identb[:, :]),
                     waits=waits, inc=[(sT, 1)])
                waits = []
            wstage_tr[slot] = sT.n
            trd = sT.n
            p = B.op("V", lambda e, d=wdt_g[:, j, :], s=tbf: e.tensor_copy(d, s),
                     waits=[(sT, trd)], inc=[(sV, 1)])
            psb_evac_v[tb] = p[sV]
            wdt_ev_v[j] = p[sV]
        wdt_ready_v = sV.n
        if debug:
            B.op("S", lambda e, dst=hTd[grp], s=big2[:, :]:
                 e.dma_start(out=dst, in_=s.rearrange("p (h t) -> p h t", h=GHT)),
                 waits=[(sV, h_ready_v)], dma_inc=[(qC, 16)])
            B.op("S", lambda e, dst=wdtd[grp], s=wdt_g[:, :, :]: e.dma_start(out=dst, in_=s),
                 waits=[(sV, wdt_ready_v)], dma_inc=[(qC, 16)])
        stt_v = [0] * NT
        for tt in range(NT):
            sl = tt % 2
            waits = list(grp_store_done)
            if tt >= 2:
                waits.append((sV, stt_v[tt - 2]))
            else:
                waits.append((sV, cinit_last_v if grp == 0 else stt_prev))
            dst = smxD[:, sl, :]
            src = out1h[tt * 128:(tt + 1) * 128, :]
            B.op("Y", lambda e, dst=dst, src=src: e.dma_start(out=dst, in_=src),
                 waits=waits, dma_inc=[(qX[sl], 16)])
            o1_ld = qX[sl].n
            waits = [(sV, wdt_ready_v), (sV, h_ready_v)]
            for db in (4, 5):
                if down_bank_stt.get(db) is not None:
                    waits.append((sV, down_bank_stt[db]))
            for j in range(GHT):
                lhsT = hT[:, j, tt * 128:(tt + 1) * 128]
                for dc in range(2):
                    rhs = wdt_g[:, j, dc * 512:(dc + 1) * 512]
                    B.op("T", lambda e, o=bank(4 + dc), lhsT=lhsT, rhs=rhs, j=j:
                         e.matmul(o, lhsT, rhs, start=(j == 0), stop=(j == GHT - 1),
                                  skip_group_check=True),
                         waits=waits, inc=[(sT, 1)])
                    waits = []
            down_done = sT.n
            for dc in range(2):
                db = 4 + dc
                p = B.op("V", lambda e, d=outstD[:, sl, dc * 512:(dc + 1) * 512],
                         s=bank(db), o1=smxD[:, sl, dc * 512:(dc + 1) * 512]:
                         e.scalar_tensor_tensor(d, s, INV127SQ, o1, ALU.mult, ALU.add),
                         waits=[(sT, down_done), (qX[sl], o1_ld), (qOU[sl], outst_dma[sl])],
                         inc=[(sV, 1)])
                down_bank_stt[db] = p[sV]
                stt_v[tt] = p[sV]
            hT_rd_t = sT.n
            dstt = (out1h if grp < NGRP - 1 else out_d)[tt * 128:(tt + 1) * 128, :]
            p = B.op("S", lambda e, dst=dstt, s=outstD[:, sl, :]: e.dma_start(out=dst, in_=s),
                     waits=[(sV, stt_v[tt])], dma_inc=[(qOU[sl], 16)])
            outst_dma[sl] = p[qOU[sl]]
        grp_store_done = [(qOU[0], qOU[0].n), (qOU[1], qOU[1].n)]
        stt_prev = sV.n

    B.op("G", lambda e: e.nop(),
         waits=[(qOU[0], qOU[0].n), (qOU[1], qOU[1].n)], inc=[(sG, 1)])

    with nc.Block() as block:
        @block.tensor
        def _(eng):
            for fn in B.streams["T"]:
                fn(eng)

        @block.vector
        def _(eng):
            for fn in B.streams["V"]:
                fn(eng)

        @block.scalar
        def _(eng):
            for fn in B.streams["S"]:
                fn(eng)

        @block.sync
        def _(eng):
            for fn in B.streams["Y"]:
                fn(eng)

        @block.gpsimd
        def _(eng):
            for fn in B.streams["G"]:
                fn(eng)

    return nc


def kernel(L0_attn_x, L0_attn_W_attn, L0_attn_b_attn,
           L0_ffn_w_gate, L0_ffn_w_up, L0_ffn_w_down):
    x = np.asarray(L0_attn_x, dtype=np.float32)
    W = np.ascontiguousarray(np.asarray(L0_attn_W_attn, dtype=np.float32))
    b = np.asarray(L0_attn_b_attn, dtype=np.float32)
    wg = np.ascontiguousarray(np.asarray(L0_ffn_w_gate, dtype=np.int32))
    wu = np.ascontiguousarray(np.asarray(L0_ffn_w_up, dtype=np.int32))
    wd = np.ascontiguousarray(np.asarray(L0_ffn_w_down, dtype=np.int32))

    bpt = np.ascontiguousarray(b.reshape(24, 128).T)
    brow = np.ascontiguousarray(b[2048:3072].reshape(1, 1024))
    identb = np.eye(128, dtype=ml_dtypes.bfloat16)
    kk = np.arange(128)[:, None]
    qq = np.arange(512)[None, :]
    masks = np.stack([
        np.where(kk + 128 * j <= qq, 0.0, -1e9).astype(ml_dtypes.bfloat16)
        for j in range(4)
    ])

    nc = build_nc(debug=globals().get("_DEBUG_BUILD", False))
    in_maps = []
    for i in range(8):
        in_maps.append({
            "x": np.ascontiguousarray(x[i]),
            "w": W, "bpt_in": bpt, "brow_in": brow,
            "wg": wg, "wu": wu, "wd": wd,
            "identb_in": identb, "masks": masks,
        })
    res = run_bass_kernel_spmd(nc, in_maps, core_ids=list(range(8)))
    if globals().get("_DEBUG_BUILD", False):
        return res
    return np.stack([np.asarray(r["out"], dtype=np.float32) for r in res.results])


# version 09
# speedup vs baseline: 1.2033x; 1.2033x over previous
import numpy as np
import ml_dtypes
import concourse.bass as bass
import concourse.mybir as mybir
from concourse.bass_utils import run_bass_kernel_spmd

F32 = mybir.dt.float32
BF16 = mybir.dt.bfloat16
I32 = mybir.dt.int32
AF = mybir.ActivationFunctionType
ALU = mybir.AluOpType
AX = mybir.AxisListType

T = 4096
C = 1024
H3 = 3072
NT = T // 128
NCH = C // 128
NP = 8
NW = T // 512
NHT = H3 // 128
NGRP = 4
GHT = NHT // NGRP
INV127 = 1.0 / 127.0
INV127SQ = 1.0 / (127.0 * 127.0)

ENGS = ("T", "V", "S", "Y", "G")


class Sem:
    def __init__(self, nc, name):
        self._cm = nc.semaphore(name)
        self.h = self._cm.__enter__()
        self.n = 0


class Builder:

    def __init__(self, nc):
        self.nc = nc
        self.streams = {e: [] for e in ENGS}
        self.known = {e: {} for e in ENGS}
        self.own = {}

    def op(self, e, fn, waits=(), inc=(), dma_inc=(), no_auto=False):
        known = self.known[e]
        waits = list(waits)
        own = self.own.get(e)
        if own is not None and not no_auto:
            waits.append((own, own.n))
        wl = []
        for (s, v) in waits:
            if v <= 0 or known.get(s, -1) >= v:
                continue
            wl.append((s.h, v))
            known[s] = max(known.get(s, -1), v)
        post = {}
        pairs = []
        for (s, n) in inc:
            s.n += n
            pairs.append((s.h, n))
            post[s] = s.n
        for (s, n) in dma_inc:
            s.n += n
            pairs.append((s.h, n))
            post[s] = s.n

        def run(eng, wl=wl, pairs=pairs, fn=fn):
            for (h, v) in wl:
                eng.wait_ge(h, v)
            inst = fn(eng)
            for (h, n) in pairs:
                inst = inst.then_inc(h, n)

        self.streams[e].append(run)
        return post


def build_nc(debug=False):
    nc = bass.Bass("TRN2", target_bir_lowering=False,
                   dynamic_dma_scratch_size=24576)
    ikind = "ExternalOutput" if debug else "Internal"

    x_d = nc.dram_tensor("x", [T, C], F32, kind="ExternalInput").ap()
    w_d = nc.dram_tensor("w", [C, H3], F32, kind="ExternalInput").ap()
    bpt_d = nc.dram_tensor("bpt_in", [128, 24], F32, kind="ExternalInput").ap()
    brow_d = nc.dram_tensor("brow_in", [1, 1024], F32, kind="ExternalInput").ap()
    wg_d = nc.dram_tensor("wg", [H3, C], I32, kind="ExternalInput").ap()
    wu_d = nc.dram_tensor("wu", [H3, C], I32, kind="ExternalInput").ap()
    wd_d = nc.dram_tensor("wd", [C, H3], I32, kind="ExternalInput").ap()
    identb_d = nc.dram_tensor("identb_in", [128, 128], BF16, kind="ExternalInput").ap()
    masks_d = nc.dram_tensor("masks", [4, 128, 512], BF16, kind="ExternalInput").ap()
    out_d = nc.dram_tensor("out", [T, C], F32, kind="ExternalOutput").ap()

    qTh = nc.dram_tensor("qTh", [C, T], BF16, kind=ikind).ap()
    kTh = nc.dram_tensor("kTh", [C, T], BF16, kind=ikind).ap()
    vh = nc.dram_tensor("vh", [128, NT, C], BF16, kind=ikind).ap()
    out1h = nc.dram_tensor("out1h", [T, C], F32, kind=ikind).ap()
    if debug:
        xnTd = nc.dram_tensor("xnTd", [NCH, 128, T], BF16, kind="ExternalOutput").ap()
        accd = nc.dram_tensor("accd", [128, NT * C], BF16, kind="ExternalOutput").ap()
        nstd = nc.dram_tensor("nstd", [NP, 1, 32], F32, kind="ExternalOutput").ap()
        negBd = nc.dram_tensor("negBd", [NP, 128, 1], F32, kind="ExternalOutput").ap()
        out1c = nc.dram_tensor("out1c", [T, C], F32, kind="ExternalOutput").ap()
        hTd = nc.dram_tensor("hTd", [NGRP, 128, GHT, T], BF16, kind="ExternalOutput").ap()
        wdtd = nc.dram_tensor("wdtd", [NGRP, 128, GHT, 1024], BF16, kind="ExternalOutput").ap()

    big0 = nc.alloc_sbuf_tensor("big0", [128, NCH, T], BF16).ap()
    big1 = nc.alloc_sbuf_tensor("big1", [128, 32768], BF16).ap()
    big2 = nc.alloc_sbuf_tensor("big2", [128, 24576], BF16).ap()
    xnstage = nc.alloc_sbuf_tensor("xnstage", [128, 1024], BF16).ap()
    wstage = nc.alloc_sbuf_tensor("wstage", [128, 2, 1024], BF16).ap()
    wgt_t = nc.alloc_sbuf_tensor("wgt_t", [128, 1024], BF16).ap()
    wut_t = nc.alloc_sbuf_tensor("wut_t", [128, 1024], BF16).ap()
    gs = nc.alloc_sbuf_tensor("gs", [128, 512], BF16).ap()
    us = nc.alloc_sbuf_tensor("us", [128, 512], BF16).ap()
    masksb = nc.alloc_sbuf_tensor("masksb", [128, 4, 512], BF16).ap()
    identb = nc.alloc_sbuf_tensor("identb", [128, 128], BF16).ap()
    ones128 = nc.alloc_sbuf_tensor("ones128", [128, 1], BF16).ap()
    ones1 = nc.alloc_sbuf_tensor("ones1", [1, 128], BF16).ap()
    bpt = nc.alloc_sbuf_tensor("bpt", [128, 24], F32).ap()
    browb = nc.alloc_sbuf_tensor("browb", [1, 1024], BF16).ap()
    stats = nc.alloc_sbuf_tensor("stats", [128, 16], F32).ap()
    epsb = nc.alloc_sbuf_tensor("epsb", [128, 1], F32).ap()
    nstat = nc.alloc_sbuf_tensor("nstat", [1, 32], F32).ap()
    negb11 = nc.alloc_sbuf_tensor("negb11", [1, 1], BF16).ap()
    negB = nc.alloc_sbuf_tensor("negB", [128, 1], F32).ap()
    rl = nc.alloc_sbuf_tensor("rl", [128, 8], F32).ap()

    acc = big1[:, 0:32768].rearrange("p (t c) -> p t c", t=NT)
    wdt_g = big1[:, 0:6144].rearrange("p (h d) -> p h d", h=GHT)
    Wv = big2[:, 0:12288].rearrange("p (c n) -> p c n", c=NCH)
    qkstage = big2[:, 12288:14336].rearrange("p (r n) -> p r n", r=4)
    kqv = big2[:, 0:16384].rearrange("p (i t) -> p i t", i=4)
    vbuf = big2[:, 16384:20544].rearrange("p (c n) -> p c n", c=NT)
    pTv = big2[:, 20544:22592].rearrange("p (i n) -> p i n", i=2)
    ovst = big2[:, 22592:23616]
    hT = big2[:, 0:24576].rearrange("p (h t) -> p h t", h=GHT)
    smxA = big1[:, 0:4096].bitcast(F32).rearrange("p (i n) -> p i n", i=2)
    smxC = big2[:, 0:4096].bitcast(F32).rearrange("p (i n) -> p i n", i=2)
    outstC = big2[:, 4096:8192].bitcast(F32).rearrange("p (i n) -> p i n", i=2)
    smxD = big1[:, 16384:20480].bitcast(F32).rearrange("p (i n) -> p i n", i=2)
    outstD = big1[:, 20480:24576].bitcast(F32).rearrange("p (i n) -> p i n", i=2)

    psA = nc.alloc_psum_tensor("psA", [128, 8, 512], F32).ap()

    def bank(i):
        return psA[:, i, :]

    def wide(i):
        return psA[:, i:i + 2, :].rearrange("p a b -> p (a b)")

    def bank_bf(i):
        return psA[:, i, :].bitcast(BF16)

    B = Builder(nc)
    qC = Sem(nc, "qC")
    qX = [Sem(nc, "qX0"), Sem(nc, "qX1")]
    qKq = [Sem(nc, "qK0"), Sem(nc, "qK1")]
    qV = Sem(nc, "qV")
    qW = Sem(nc, "qW")
    qWS = [Sem(nc, "qWS0"), Sem(nc, "qWS1")]
    qST = [Sem(nc, "qS0"), Sem(nc, "qS1"), Sem(nc, "qS2"), Sem(nc, "qS3")]
    qOU = [Sem(nc, "qO0"), Sem(nc, "qO1")]
    sT = Sem(nc, "sT")
    sV = Sem(nc, "sV")
    sS = Sem(nc, "sS")
    sG = Sem(nc, "sG")
    B.own = {"V": sV, "S": sS, "G": sG}

    B.op("Y", lambda e: e.dma_start(out=identb[:, :], in_=identb_d[:, :]), dma_inc=[(qC, 16)])
    for j in range(4):
        dst = masksb[:, j, :]
        src = masks_d[j, :, :]
        B.op("Y", lambda e, dst=dst, src=src: e.dma_start(out=dst, in_=src), dma_inc=[(qC, 16)])
    B.op("Y", lambda e: e.dma_start(out=bpt[:, :], in_=bpt_d[:, :]), dma_inc=[(qC, 16)])
    B.op("G", lambda e: e.dma_start(out=browb[:, :], in_=brow_d[:, :]), dma_inc=[(qC, 16)])
    const_ld = qC.n
    B.op("G", lambda e: e.memset(ones128[:, :], 1.0), inc=[(sG, 1)])
    B.op("G", lambda e: e.memset(ones1[:, :], 1.0), inc=[(sG, 1)])
    B.op("G", lambda e: e.memset(epsb[:, :], 1e-5), inc=[(sG, 1)])
    B.op("G", lambda e: e.memset(vbuf[:, :, 64:65], 1.0), inc=[(sG, 1)])
    B.op("G", lambda e: e.memset(vbuf[:, :, 129:130], 1.0), inc=[(sG, 1)])
    const_ms = sG.n


    a_apply_v = [0] * NT
    a_sq_s = [0] * NT
    a_tr_t = [0] * NT
    a_evac_v = [0] * NT
    a_xld = [0] * NT

    for t in range(NT):
        sl = t % 2
        o = (t % 2) * 8
        xs = smxA[:, sl, :]
        xsrc = x_d[t * 128:(t + 1) * 128, :]
        waits = []
        if t >= 2:
            waits = [(sV, a_apply_v[t - 2]), (sS, a_sq_s[t - 2])]
        B.op("Y", lambda e, xs=xs, xsrc=xsrc: e.dma_start(out=xs, in_=xsrc),
             waits=waits, dma_inc=[(qX[sl], 16)])
        a_xld[t] = qX[sl].n

        B.op("V", lambda e, xs=xs, d=stats[:, o + 0:o + 1]: e.reduce_sum(d, xs, axis=AX.X),
             waits=[(qX[sl], a_xld[t])], inc=[(sV, 1)])
        B.op("V", lambda e, s=stats[:, o + 0:o + 1], d=stats[:, o + 1:o + 2]:
             e.tensor_scalar(d, s, 1.0 / C, None, ALU.mult), inc=[(sV, 1)])
        p = B.op("S", lambda e, xs=xs, d=pTv[:, 0, :], a=stats[:, o + 2:o + 3]:
                 e.activation(d, xs, AF.Square, accum_out=a),
                 waits=[(qX[sl], a_xld[t])], inc=[(sS, 1)])
        a_sq_s[t] = p[sS]
        B.op("V", lambda e, m=stats[:, o + 1:o + 2], d=stats[:, o + 3:o + 4]:
             e.tensor_mul(d, m, m), inc=[(sV, 1)])
        B.op("V", lambda e, ssq=stats[:, o + 2:o + 3], msq=stats[:, o + 3:o + 4],
             d=stats[:, o + 4:o + 5]:
             e.tensor_scalar(d, ssq, 1.0 / C, msq, ALU.mult, ALU.subtract),
             waits=[(sS, a_sq_s[t])], inc=[(sV, 1)])
        pv = sV.n
        B.op("S", lambda e, v=stats[:, o + 4:o + 5], d=stats[:, o + 5:o + 6]:
             e.activation(d, v, AF.Sqrt, bias=epsb[:, 0:1]),
             waits=[(sV, pv), (sG, const_ms)], inc=[(sS, 1)])
        ps = sS.n
        B.op("V", lambda e, s=stats[:, o + 5:o + 6], d=stats[:, o + 6:o + 7]:
             e.reciprocal(d, s), waits=[(sS, ps)], inc=[(sV, 1)])
        waits = [(sT, a_tr_t[t - 1])] if t >= 1 else []
        p = B.op("V", lambda e, xs=xs, m=stats[:, o + 1:o + 2], r=stats[:, o + 6:o + 7]:
                 e.tensor_scalar(xnstage[:, :], xs, m, r, ALU.subtract, ALU.mult),
                 waits=waits, inc=[(sV, 1)])
        a_apply_v[t] = p[sV]
        tb = bank_bf(6 + t % 2)
        waits = [(sV, a_apply_v[t]), (qC, const_ld)]
        if t >= 2:
            waits.append((sV, a_evac_v[t - 2]))
        for c in range(NCH):
            dst = tb[:, c * 128:(c + 1) * 128]
            src = xnstage[:, c * 128:(c + 1) * 128]
            p = B.op("T", lambda e, dst=dst, src=src: e.transpose(dst, src, identb[:, :]),
                     waits=waits, inc=[(sT, 1)])
            waits = []
        a_tr_t[t] = p[sT]
        dstx = big0[:, :, t * 128:(t + 1) * 128]
        srcx = tb.rearrange("p (c w) -> p c w", c=NCH)
        p = B.op("V", lambda e, dstx=dstx, srcx=srcx: e.tensor_copy(dstx, srcx),
                 waits=[(sT, a_tr_t[t])], inc=[(sV, 1)])
        a_evac_v[t] = p[sV]
    a_end_v = sV.n
    if debug:
        for c in range(NCH):
            dst = xnTd[c]
            src = big0[:, c, :]
            B.op("S", lambda e, dst=dst, src=src: e.dma_start(out=dst, in_=src),
                 waits=[(sV, a_end_v)], dma_inc=[(qC, 16)])

    b_cast = [0, 0]
    qk_bank_evac = {}
    stage_dma = [0, 0, 0, 0]
    bmm_i = 0

    for half in range(2):
        waits = [] if half == 0 else [(sT, sT.n)]
        for c in range(NCH):
            dst = Wv[:, c, :]
            src = w_d[c * 128:(c + 1) * 128, half * 1536:(half + 1) * 1536]
            B.op("G", lambda e, dst=dst, src=src: e.dma_start(out=dst, in_=src),
                 waits=waits, dma_inc=[(qW, 16)])
            waits = []
        b_cast[half] = qW.n

        qk_tiles = list(range(12)) if half == 0 else list(range(4))
        for fl in qk_tiles:
            ft = half * 12 + fl
            for tw in range(NW):
                bb2 = bmm_i % 2
                bmm_i += 1
                for c in range(NCH):
                    lhsT = Wv[:, c, fl * 128:(fl + 1) * 128]
                    rhs = big0[:, c, tw * 512:(tw + 1) * 512]
                    waits = [(qW, b_cast[half]), (sV, a_end_v)]
                    if c == 0 and bb2 in qk_bank_evac:
                        waits.append((sV, qk_bank_evac[bb2]))
                    B.op("T", lambda e, lhsT=lhsT, rhs=rhs, o=bank(bb2), c=c:
                         e.matmul(o, lhsT, rhs, start=(c == 0), stop=(c == NCH - 1)),
                         waits=waits, inc=[(sT, 1)])
                mm_done = sT.n
                r = (ft * NW + tw) % 4
                p = B.op("V", lambda e, d=qkstage[:, r, :], s=bank(bb2), bc=bpt[:, ft:ft + 1]:
                         e.tensor_scalar(d, s, bc, None, ALU.add),
                         waits=[(sT, mm_done), (qC, const_ld), (qST[r], stage_dma[r])],
                         inc=[(sV, 1)])
                qk_bank_evac[bb2] = p[sV]
                ev = p[sV]
                if ft < 8:
                    dst = qTh[ft * 128:(ft + 1) * 128, tw * 512:(tw + 1) * 512]
                else:
                    dst = kTh[(ft - 8) * 128:(ft - 8 + 1) * 128, tw * 512:(tw + 1) * 512]
                p = B.op("S", lambda e, dst=dst, s=qkstage[:, r, :]: e.dma_start(out=dst, in_=s),
                         waits=[(sV, ev)], dma_inc=[(qST[r], 16)])
                stage_dma[r] = p[qST[r]]

    v_bank_evac = {}
    for tt in range(NT):
        for vc in range(2):
            bb2 = 2 + (tt * 2 + vc) % 2
            for c in range(NCH):
                lhsT = big0[:, c, tt * 128:(tt + 1) * 128]
                rhs = Wv[:, c, 512 + vc * 512:512 + (vc + 1) * 512]
                waits = [(qW, b_cast[1])]
                if c == 0 and bb2 in v_bank_evac:
                    waits.append((sV, v_bank_evac[bb2]))
                B.op("T", lambda e, lhsT=lhsT, rhs=rhs, o=bank(bb2), c=c:
                     e.matmul(o, lhsT, rhs, start=(c == 0), stop=False),
                     waits=waits, inc=[(sT, 1)])
            B.op("T", lambda e, o=bank(bb2), r=browb[0:1, vc * 512:(vc + 1) * 512]:
                 e.matmul(o, ones1[:, :], r, start=False, stop=True),
                 waits=[(qC, const_ld), (sG, const_ms)], inc=[(sT, 1)])
            mm_done = sT.n
            r = (tt * 2 + vc) % 4
            p = B.op("V", lambda e, d=qkstage[:, r, :], s=bank(bb2):
                     e.tensor_copy(d, s),
                     waits=[(sT, mm_done), (qST[r], stage_dma[r])], inc=[(sV, 1)])
            v_bank_evac[bb2] = p[sV]
            ev = p[sV]
            dst = vh[:, tt, vc * 512:(vc + 1) * 512]
            p = B.op("S", lambda e, dst=dst, s=qkstage[:, r, :]: e.dma_start(out=dst, in_=s),
                     waits=[(sV, ev)], dma_inc=[(qST[r], 16)])
            stage_dma[r] = p[qST[r]]
    b_end_t = sT.n
    b_end_v = sV.n
    b_stores = [(qST[r], qST[r].n) for r in range(4)]

    kq_ld = [0, 0]
    kq_last_t = [b_end_t, b_end_t]
    v_ld = 0
    pv_last_t = b_end_t
    exp_done = {}
    pT_rd = {}
    psT_mult_v = {6: b_end_v, 7: b_end_v}
    ov_evac_v = b_end_v
    acc_done_v = b_end_v

    def load_pair(pr):
        nonlocal v_ld
        pb = pr % 2
        waits = list(b_stores) + [(sT, kq_last_t[pb])]
        dst = kqv[:, 2 * pb, :]
        src = kTh[pr * 128:(pr + 1) * 128, :]
        B.op("Y", lambda e, dst=dst, src=src: e.dma_start(out=dst, in_=src),
             waits=waits, dma_inc=[(qKq[pb], 16)])
        dst = kqv[:, 2 * pb + 1, :]
        src = qTh[pr * 128:(pr + 1) * 128, :]
        B.op("Y", lambda e, dst=dst, src=src: e.dma_start(out=dst, in_=src),
             dma_inc=[(qKq[pb], 16)])
        kq_ld[pb] = qKq[pb].n

    def load_v(pr):
        nonlocal v_ld
        waits = list(b_stores) + [(sT, pv_last_t), (sG, const_ms)]
        for hh in range(2):
            h = 2 * pr + hh
            off = 0 if hh == 0 else 65
            for cq in range(4):
                dst = vbuf[:, cq * 8:(cq + 1) * 8, off:off + 64]
                src = vh[:, cq * 8:(cq + 1) * 8, h * 64:(h + 1) * 64]
                B.op("Y", lambda e, dst=dst, src=src: e.dma_start(out=dst, in_=src),
                     waits=waits, dma_inc=[(qV, 16)])
                waits = []
        v_ld = qV.n

    load_pair(0)
    load_v(0)

    for pr in range(NP):
        pb = pr % 2
        kqk = kqv[:, 2 * pb, :]
        kqq = kqv[:, 2 * pb + 1, :]

        sq_mm_t = {}
        red_v = {}
        for it in range(16):
            ch = it % 8
            src = (kqq if it < 8 else kqk)[:, ch * 512:(ch + 1) * 512]
            sqo = pTv[:, it % 2, 0:512]
            waits = [(qKq[pb], kq_ld[pb])]
            if it >= 2:
                waits.append((sT, sq_mm_t[it - 2]))
            else:
                waits.append((sT, pv_last_t))
            B.op("S", lambda e, d=sqo, s=src: e.activation(d, s, AF.Square),
                 waits=waits, inc=[(sS, 1)])
            sq_done = sS.n
            nb = 6 + it % 2
            waits = [(sS, sq_done), (sV, psT_mult_v[nb]), (sG, const_ms)]
            if it >= 2:
                waits.append((sV, red_v[it - 2]))
            p = B.op("T", lambda e, o=bank(nb)[0:1, :], s=sqo:
                     e.matmul(o, ones128[:, :], s, start=True, stop=True),
                     waits=waits, inc=[(sT, 1)])
            sq_mm_t[it] = p[sT]
            dcol = nstat[0:1, it:it + 1]
            p = B.op("V", lambda e, d=dcol, s=bank(nb)[0:1, :]:
                     e.reduce_max(d, s, axis=AX.X),
                     waits=[(sT, sq_mm_t[it])], inc=[(sV, 1)])
            red_v[it] = p[sV]
        psT_mult_v[6] = red_v[14]
        psT_mult_v[7] = red_v[15]
        B.op("V", lambda e: e.reduce_max(nstat[0:1, 16:17], nstat[0:1, 0:8], axis=AX.X),
             inc=[(sV, 1)])
        B.op("V", lambda e: e.reduce_max(nstat[0:1, 17:18], nstat[0:1, 8:16], axis=AX.X),
             inc=[(sV, 1)])
        B.op("V", lambda e: e.tensor_mul(nstat[0:1, 18:19], nstat[0:1, 16:17],
                                         nstat[0:1, 17:18]), inc=[(sV, 1)])
        pv_ = sV.n
        B.op("S", lambda e: e.activation(nstat[0:1, 19:20], nstat[0:1, 18:19],
                                         AF.Sqrt, bias=epsb[0:1, 0:1], scale=1.0 / 64.0),
             waits=[(sV, pv_)], inc=[(sS, 1)])
        ps_ = sS.n
        B.op("V", lambda e: e.tensor_scalar(negb11[:, :], nstat[0:1, 19:20], -1.0,
                                            None, ALU.mult),
             waits=[(sS, ps_)], inc=[(sV, 1)])
        nv_ = sV.n
        B.op("T", lambda e, o=psA[:, 6, 0:1]: e.matmul(o, ones1[:, :], negb11[:, :],
                                                       start=True, stop=True),
             waits=[(sV, nv_)], inc=[(sT, 1)])
        nt_ = sT.n
        p = B.op("V", lambda e, s=psA[:, 6, 0:1]: e.tensor_copy(negB[:, :], s),
                 waits=[(sT, nt_)], inc=[(sV, 1)])
        negB_v = p[sV]
        psT_mult_v[6] = p[sV]
        if debug:
            B.op("S", lambda e, dst=nstd[pr], s=nstat[:, :]: e.dma_start(out=dst, in_=s),
                 waits=[(sV, negB_v)], dma_inc=[(qC, 16)])
            B.op("S", lambda e, dst=negBd[pr], s=negB[:, :]: e.dma_start(out=dst, in_=s),
                 waits=[(sV, negB_v)], dma_inc=[(qC, 16)])

        if pr + 1 < NP:
            load_pair(pr + 1)

        for w in range(NW):
            nk = 4 * (w + 1)
            pv_mm_t = {}
            smm_t = {}

            def emit_smm(ki, w=w, nk=nk, kqk=kqk, kqq=kqq, pb=pb):
                par = ki % 2
                waits = [(qKq[pb], kq_ld[pb])]
                if exp_done.get((w, ki - 2)) is not None:
                    waits.append((sS, exp_done[(w, ki - 2)]))
                elif ki < 2 and exp_done.get("last") is not None:
                    waits.append((sS, exp_done["last"]))
                diag = ki >= 4 * w
                for hh, bx in ((0, 2 * par), (1, 2 * par + 1)):
                    r0 = hh * 64
                    lhsT = kqk[r0:r0 + 64, ki * 128:(ki + 1) * 128]
                    rhs = kqq[r0:r0 + 64, w * 512:(w + 1) * 512]
                    B.op("T", lambda e, o=bank(bx), lhsT=lhsT, rhs=rhs, diag=diag:
                         e.matmul(o, lhsT, rhs, start=True, stop=(not diag)),
                         waits=waits, inc=[(sT, 1)])
                    waits = []
                    if diag:
                        j = ki - 4 * w
                        B.op("T", lambda e, o=bank(bx), m=masksb[:, j, :]:
                             e.matmul(o, identb[:, :], m, start=False, stop=True),
                             inc=[(sT, 1)])
                smm_t[ki] = sT.n

            for ki in range(min(2, nk)):
                emit_smm(ki)
            for ki in range(nk):
                par = ki % 2
                waits = [(sT, smm_t[ki]), (sV, negB_v)]
                if pT_rd.get((w, ki - 2)) is not None:
                    waits.append((sT, pT_rd[(w, ki - 2)]))
                elif ki < 2 and pT_rd.get("last") is not None:
                    waits.append((sT, pT_rd["last"]))
                p = B.op("S", lambda e, d=pTv[:, par, :], s=wide(2 * par):
                         e.activation(d, s, AF.Exp, bias=negB[:, 0:1], scale=0.125),
                         waits=waits, inc=[(sS, 1)], no_auto=True)
                exp_done[(w, ki)] = p[sS]
                waits = [(sS, exp_done[(w, ki)]), (qV, v_ld)]
                if ki == 0:
                    waits.append((sV, ov_evac_v))
                for hh, ob in ((0, 4), (1, 5)):
                    lhsT = vbuf[:, ki, hh * 65:(hh + 1) * 65]
                    rhs = pTv[:, par, hh * 512:(hh + 1) * 512]
                    B.op("T", lambda e, o=bank(ob)[0:65, :], lhsT=lhsT, rhs=rhs,
                         ki=ki, nk=nk:
                         e.matmul(o, lhsT, rhs, start=(ki == 0), stop=(ki == nk - 1)),
                         waits=waits, inc=[(sT, 1)])
                    waits = []
                pv_mm_t[ki] = sT.n
                pT_rd[(w, ki)] = sT.n
                if ki + 2 < nk:
                    emit_smm(ki + 2)
            pv_last_t = pv_mm_t[nk - 1]
            pT_rd["last"] = pv_last_t
            exp_done["last"] = exp_done[(w, nk - 1)]

            for hh, ob in ((0, 4), (1, 5)):
                dst = ovst[0:65, hh * 512:(hh + 1) * 512]
                B.op("V", lambda e, dst=dst, s=bank(ob)[0:65, :]: e.tensor_copy(dst, s),
                     waits=[(sT, pv_last_t)], inc=[(sV, 1)])
            ov_evac_v = sV.n
            tb = 6 + w % 2
            tbf = bank_bf(tb)
            waits = [(sV, ov_evac_v), (sV, psT_mult_v[tb])]
            for s8 in range(8):
                hh, bcol = s8 // 4, s8 % 4
                src = ovst[0:65, hh * 512 + bcol * 128: hh * 512 + (bcol + 1) * 128]
                dst = tbf[:, s8 * 66:s8 * 66 + 65]
                B.op("T", lambda e, dst=dst, src=src:
                     e.transpose(dst, src, identb[0:65, 0:65]),
                     waits=waits, inc=[(sT, 1)])
                waits = []
            tr_done = sT.n
            B.op("V", lambda e, s=tbf[:, 64:64 + 8 * 66:66]: e.reciprocal(rl[:, 0:8], s),
                 waits=[(sT, tr_done)], inc=[(sV, 1)])
            for s8 in range(8):
                hh, bcol = s8 // 4, s8 % 4
                h = 2 * pr + hh
                tt = 4 * w + bcol
                dst = acc[:, tt, h * 64:(h + 1) * 64]
                src = tbf[:, s8 * 66:s8 * 66 + 64]
                B.op("V", lambda e, dst=dst, src=src, r=rl[:, s8:s8 + 1]:
                     e.tensor_scalar(dst, src, r, None, ALU.mult), inc=[(sV, 1)])
            psT_mult_v[tb] = sV.n
        kq_last_t[pb] = pv_last_t

        if pr + 1 < NP:
            load_v(pr + 1)
    c_end_t = sT.n
    c_end_s = sS.n
    acc_done_v = sV.n
    if debug:
        B.op("S", lambda e: e.dma_start(out=accd[:, :], in_=big1[:, :]),
             waits=[(sV, acc_done_v)], dma_inc=[(qC, 16)])

    add_v = [0] * NT
    outst_dma = [qOU[0].n, qOU[1].n]
    smx_dma = [0, 0]
    for tt in range(NT):
        sl = tt % 2
        waits = [(sV, acc_done_v), (sT, c_end_t), (sS, c_end_s)] if tt < 2 else [(sV, add_v[tt - 2])]
        dst = smxC[:, sl, :]
        src = x_d[tt * 128:(tt + 1) * 128, :]
        B.op("Y", lambda e, dst=dst, src=src: e.dma_start(out=dst, in_=src),
             waits=waits, dma_inc=[(qX[sl], 16)])
        smx_dma[sl] = qX[sl].n
        p = B.op("V", lambda e, d=outstC[:, sl, :], a=smxC[:, sl, :], b=acc[:, tt, :]:
                 e.tensor_tensor(d, a, b, ALU.add),
                 waits=[(qX[sl], smx_dma[sl]), (qOU[sl], outst_dma[sl])], inc=[(sV, 1)])
        add_v[tt] = p[sV]
        dst = out1h[tt * 128:(tt + 1) * 128, :]
        p = B.op("S", lambda e, dst=dst, s=outstC[:, sl, :]: e.dma_start(out=dst, in_=s),
                 waits=[(sV, add_v[tt])], dma_inc=[(qOU[sl], 16)])
        outst_dma[sl] = p[qOU[sl]]
    cinit_last_v = sV.n
    grp_store_done = [(qOU[0], qOU[0].n), (qOU[1], qOU[1].n)]
    if debug:
        B.op("Y", lambda e: e.dma_start(out=out1c[:, :], in_=out1h[:, :]),
             waits=list(grp_store_done), dma_inc=[(qC, 16)])

    wstage_tr = [c_end_t, c_end_t]
    psb_evac_v = {6: cinit_last_v, 7: cinit_last_v}
    hT_rd_t = c_end_t
    gu_bank_silu = {}
    gu_bank_uev = {}
    down_bank_stt = {}
    silu_done = 0
    hmul_done = {}
    wgt_ev_v = 0
    wut_ev_v = 0
    wdt_ev_v = [0] * GHT
    gu_last_t = {}

    for grp in range(NGRP):
        for j in range(GHT):
            ht = grp * GHT + j
            waits = [(sT, wstage_tr[0])]
            dst = wstage[:, 0, :]
            src = wg_d[ht * 128:(ht + 1) * 128, :]
            B.op("G", lambda e, dst=dst, src=src: e.dma_start(out=dst, in_=src),
                 waits=waits, dma_inc=[(qWS[0], 16)])
            wg_cast = qWS[0].n
            dst = wstage[:, 1, :]
            src = wu_d[ht * 128:(ht + 1) * 128, :]
            B.op("G", lambda e, dst=dst, src=src: e.dma_start(out=dst, in_=src),
                 waits=[(sT, wstage_tr[1])], dma_inc=[(qWS[1], 16)])
            wu_cast = qWS[1].n
            for (slot, cast_q, dstt, evv) in ((0, wg_cast, wgt_t, "g"), (1, wu_cast, wut_t, "u")):
                tb = 6 + slot
                tbf = bank_bf(tb)
                waits = [(qWS[slot], cast_q), (sV, psb_evac_v[tb]),
                         (sV, wgt_ev_v if slot == 0 else wut_ev_v)]
                if j > 0 or grp > 0:
                    waits.append((sT, gu_last_t.get(evv, 0)))
                for dc in range(NCH):
                    dstp = tbf[:, dc * 128:(dc + 1) * 128]
                    srcp = wstage[:, slot, dc * 128:(dc + 1) * 128]
                    B.op("T", lambda e, dstp=dstp, srcp=srcp:
                         e.transpose(dstp, srcp, identb[:, :]),
                         waits=waits, inc=[(sT, 1)])
                    waits = []
                wstage_tr[slot] = sT.n
                trd = sT.n
                p = B.op("V", lambda e, d=dstt, s=tbf: e.tensor_copy(d, s),
                         waits=[(sT, trd)], inc=[(sV, 1)])
                psb_evac_v[tb] = p[sV]
                if slot == 0:
                    wgt_ev_v = p[sV]
                else:
                    wut_ev_v = p[sV]
            for tw in range(NW):
                gb = (tw % 2) * 2
                ub = gb + 1
                for (bx, wt, wev) in ((gb, wgt_t, wgt_ev_v), (ub, wut_t, wut_ev_v)):
                    waits = [(sV, wev)]
                    if gu_bank_silu.get(bx) is not None:
                        waits.append((sS, gu_bank_silu[bx]))
                    if gu_bank_uev.get(bx) is not None:
                        waits.append((sV, gu_bank_uev[bx]))
                    for c in range(NCH):
                        lhsT = wt[:, c * 128:(c + 1) * 128]
                        rhs = big0[:, c, tw * 512:(tw + 1) * 512]
                        B.op("T", lambda e, o=bank(bx), lhsT=lhsT, rhs=rhs, c=c:
                             e.matmul(o, lhsT, rhs, start=(c == 0), stop=(c == NCH - 1)),
                             waits=waits, inc=[(sT, 1)])
                        waits = []
                gu_last_t["g"] = sT.n
                gu_last_t["u"] = sT.n
                gu_done = sT.n
                waits = [(sT, gu_done)]
                if hmul_done.get("v") is not None:
                    waits.append((sV, hmul_done["v"]))
                p = B.op("S", lambda e, s=bank(gb): e.activation(gs[:, :], s, AF.Silu,
                                                                 scale=INV127),
                         waits=waits, inc=[(sS, 1)])
                gu_bank_silu[gb] = p[sS]
                silu_done = p[sS]
                waits = [(sT, gu_done)]
                if hmul_done.get("v") is not None:
                    waits.append((sV, hmul_done["v"]))
                if tw == 0 and j == 0:
                    if grp > 0:
                        waits.append((sT, hT_rd_t))
                    waits.append((qOU[0], outst_dma[0]))
                    waits.append((qOU[1], outst_dma[1]))
                p = B.op("V", lambda e, s=bank(ub): e.tensor_copy(us[:, :], s),
                         waits=waits, inc=[(sV, 1)])
                gu_bank_uev[ub] = p[sV]
                dsth = hT[:, j, tw * 512:(tw + 1) * 512]
                p = B.op("V", lambda e, d=dsth: e.tensor_tensor(d, gs[:, :], us[:, :],
                                                                ALU.mult),
                         waits=[(sS, silu_done)], inc=[(sV, 1)])
                hmul_done["v"] = p[sV]
        h_ready_v = sV.n
        wd_r = wd_d.rearrange("(dc p) h -> p dc h", p=128)
        for j in range(GHT):
            hc = grp * GHT + j
            slot = j % 2
            waits = [(sT, wstage_tr[slot])]
            for dcc in range(NCH):
                dst = wstage[:, slot, dcc * 128:(dcc + 1) * 128]
                src = wd_r[:, dcc, hc * 128:(hc + 1) * 128]
                B.op("G", lambda e, dst=dst, src=src: e.dma_start(out=dst, in_=src),
                     waits=waits, dma_inc=[(qWS[slot], 16)])
                waits = []
            wd_cast = qWS[slot].n
            tb = 6 + slot
            tbf = bank_bf(tb)
            waits = [(qWS[slot], wd_cast), (sV, psb_evac_v[tb]), (sV, wdt_ev_v[j])]
            if grp > 0:
                waits.append((sT, hT_rd_t))
            for dc in range(NCH):
                dstp = tbf[:, dc * 128:(dc + 1) * 128]
                srcp = wstage[:, slot, dc * 128:(dc + 1) * 128]
                B.op("T", lambda e, dstp=dstp, srcp=srcp:
                     e.transpose(dstp, srcp, identb[:, :]),
                     waits=waits, inc=[(sT, 1)])
                waits = []
            wstage_tr[slot] = sT.n
            trd = sT.n
            p = B.op("V", lambda e, d=wdt_g[:, j, :], s=tbf: e.tensor_copy(d, s),
                     waits=[(sT, trd)], inc=[(sV, 1)])
            psb_evac_v[tb] = p[sV]
            wdt_ev_v[j] = p[sV]
        wdt_ready_v = sV.n
        if debug:
            B.op("S", lambda e, dst=hTd[grp], s=big2[:, :]:
                 e.dma_start(out=dst, in_=s.rearrange("p (h t) -> p h t", h=GHT)),
                 waits=[(sV, h_ready_v)], dma_inc=[(qC, 16)])
            B.op("S", lambda e, dst=wdtd[grp], s=wdt_g[:, :, :]: e.dma_start(out=dst, in_=s),
                 waits=[(sV, wdt_ready_v)], dma_inc=[(qC, 16)])
        stt_v = [0] * NT
        for tt in range(NT):
            sl = tt % 2
            waits = list(grp_store_done)
            if tt >= 2:
                waits.append((sV, stt_v[tt - 2]))
            else:
                waits.append((sV, cinit_last_v if grp == 0 else stt_prev))
            dst = smxD[:, sl, :]
            src = out1h[tt * 128:(tt + 1) * 128, :]
            B.op("Y", lambda e, dst=dst, src=src: e.dma_start(out=dst, in_=src),
                 waits=waits, dma_inc=[(qX[sl], 16)])
            o1_ld = qX[sl].n
            for dc in range(2):
                db = 4 + (tt * 2 + dc) % 2
                waits = [(sV, wdt_ready_v), (sV, h_ready_v)]
                if down_bank_stt.get(db) is not None:
                    waits.append((sV, down_bank_stt[db]))
                for j in range(GHT):
                    lhsT = hT[:, j, tt * 128:(tt + 1) * 128]
                    rhs = wdt_g[:, j, dc * 512:(dc + 1) * 512]
                    B.op("T", lambda e, o=bank(db), lhsT=lhsT, rhs=rhs, j=j:
                         e.matmul(o, lhsT, rhs, start=(j == 0), stop=(j == GHT - 1)),
                         waits=waits, inc=[(sT, 1)])
                    waits = []
                down_done = sT.n
                p = B.op("V", lambda e, d=outstD[:, sl, dc * 512:(dc + 1) * 512],
                         s=bank(db), o1=smxD[:, sl, dc * 512:(dc + 1) * 512]:
                         e.scalar_tensor_tensor(d, s, INV127SQ, o1, ALU.mult, ALU.add),
                         waits=[(sT, down_done), (qX[sl], o1_ld), (qOU[sl], outst_dma[sl])],
                         inc=[(sV, 1)])
                down_bank_stt[db] = p[sV]
                stt_v[tt] = p[sV]
            hT_rd_t = sT.n
            dstt = (out1h if grp < NGRP - 1 else out_d)[tt * 128:(tt + 1) * 128, :]
            p = B.op("S", lambda e, dst=dstt, s=outstD[:, sl, :]: e.dma_start(out=dst, in_=s),
                     waits=[(sV, stt_v[tt])], dma_inc=[(qOU[sl], 16)])
            outst_dma[sl] = p[qOU[sl]]
        grp_store_done = [(qOU[0], qOU[0].n), (qOU[1], qOU[1].n)]
        stt_prev = sV.n

    B.op("G", lambda e: e.nop(),
         waits=[(qOU[0], qOU[0].n), (qOU[1], qOU[1].n)], inc=[(sG, 1)])

    with nc.Block() as block:
        @block.tensor
        def _(eng):
            for fn in B.streams["T"]:
                fn(eng)

        @block.vector
        def _(eng):
            for fn in B.streams["V"]:
                fn(eng)

        @block.scalar
        def _(eng):
            for fn in B.streams["S"]:
                fn(eng)

        @block.sync
        def _(eng):
            for fn in B.streams["Y"]:
                fn(eng)

        @block.gpsimd
        def _(eng):
            for fn in B.streams["G"]:
                fn(eng)

    return nc


def kernel(L0_attn_x, L0_attn_W_attn, L0_attn_b_attn,
           L0_ffn_w_gate, L0_ffn_w_up, L0_ffn_w_down):
    x = np.asarray(L0_attn_x, dtype=np.float32)
    W = np.ascontiguousarray(np.asarray(L0_attn_W_attn, dtype=np.float32))
    b = np.asarray(L0_attn_b_attn, dtype=np.float32)
    wg = np.ascontiguousarray(np.asarray(L0_ffn_w_gate, dtype=np.int32))
    wu = np.ascontiguousarray(np.asarray(L0_ffn_w_up, dtype=np.int32))
    wd = np.ascontiguousarray(np.asarray(L0_ffn_w_down, dtype=np.int32))

    bpt = np.ascontiguousarray(b.reshape(24, 128).T)
    brow = np.ascontiguousarray(b[2048:3072].reshape(1, 1024))
    identb = np.eye(128, dtype=ml_dtypes.bfloat16)
    kk = np.arange(128)[:, None]
    qq = np.arange(512)[None, :]
    masks = np.stack([
        np.where(kk + 128 * j <= qq, 0.0, -1e9).astype(ml_dtypes.bfloat16)
        for j in range(4)
    ])

    nc = build_nc(debug=globals().get("_DEBUG_BUILD", False))
    in_maps = []
    for i in range(8):
        in_maps.append({
            "x": np.ascontiguousarray(x[i]),
            "w": W, "bpt_in": bpt, "brow_in": brow,
            "wg": wg, "wu": wu, "wd": wd,
            "identb_in": identb, "masks": masks,
        })
    res = run_bass_kernel_spmd(nc, in_maps, core_ids=list(range(8)))
    if globals().get("_DEBUG_BUILD", False):
        return res
    return np.stack([np.asarray(r["out"], dtype=np.float32) for r in res.results])
